# version 05
# baseline (speedup 1.0000x reference)
import numpy as np
from contextlib import ExitStack

import concourse.bass as bass
import concourse.mybir as mybir
from concourse.bass_utils import run_bass_kernel_spmd

F32 = mybir.dt.float32
BF16 = mybir.dt.bfloat16
AF = mybir.ActivationFunctionType
ALU = mybir.AluOpType

P = 128
D = 1024
H = 4096
SEQ = 2048
BCORE = 2
NCORES = 8
TT = 512
STEPS = 256
DK = D // P
HK = H // P
HM = H // P
DM = D // P
NCH = TT // P
NT = SEQ // STEPS
EPS = 1e-5

ENGINES = ("sync", "gpsimd", "scalar", "vector", "tensor")


ENG_SEM = {"vector": "dve", "scalar": "act"}


class Sched:
    def __init__(self, semnames):
        self.cnt = {s: 0 for s in semnames}
        self.ops = {e: [] for e in ENGINES}
        self.wm = {e: {} for e in ENGINES}

    def emit(self, eng, fn, waits=(), inc=None, inc_by=1):
        own = ENG_SEM.get(eng)
        if own is not None and self.cnt[own] > 0:
            waits = list(waits) + [(own, self.cnt[own], "auto", "s")]
        wl = []
        for tok in waits:
            if tok is None:
                continue
            s, v, peng, haz = tok
            if peng == eng and haz == "t":
                continue
            if self.wm[eng].get(s, 0) >= v:
                continue
            self.wm[eng][s] = v
            wl.append((s, v))
        tokout = None
        if inc is not None:
            self.cnt[inc] += inc_by
            tokout = (inc, self.cnt[inc], eng, "t")
        self.ops[eng].append((wl, fn, inc, inc_by))
        return tokout


def sc(tok):
    if tok is None:
        return None
    s, v, e, _ = tok
    return (s, v, e, "s")


def dh(tok):
    if tok is None:
        return None
    s, v, e, _ = tok
    return (s, v, e, "d")


def build_program(nt, dbg=False):
    nc = bass.Bass("TRN2", target_bir_lowering=False)

    x_in = nc.dram_tensor("x", [BCORE, SEQ, D], F32, kind="ExternalInput")
    pk1 = nc.dram_tensor("pk1", [HM, P, 2 * DK * P], F32, kind="ExternalInput")
    pk2 = nc.dram_tensor("pk2", [HM, P, HK * P], F32, kind="ExternalInput")
    pk3 = nc.dram_tensor("pk3", [HM, P, HK * P], F32, kind="ExternalInput")
    pk4 = nc.dram_tensor("pk4", [DM, P, HK * P], F32, kind="ExternalInput")
    s1_in = nc.dram_tensor("s1", [P, HM], F32, kind="ExternalInput")
    s2_in = nc.dram_tensor("s2", [P, HM], F32, kind="ExternalInput")
    sh1_in = nc.dram_tensor("sh1", [P, HM], F32, kind="ExternalInput")
    sh2_in = nc.dram_tensor("sh2", [P, HM], F32, kind="ExternalInput")
    s3_in = nc.dram_tensor("s3", [P, DM], F32, kind="ExternalInput")
    at_in = nc.dram_tensor("at", [P, P], F32, kind="ExternalInput")
    bt_in = nc.dram_tensor("bt", [P, DK * P], F32, kind="ExternalInput")
    ct_in = nc.dram_tensor("ct", [P, D], F32, kind="ExternalInput")
    dv_in = nc.dram_tensor("dv", [1, D], F32, kind="ExternalInput")
    gv_in = nc.dram_tensor("gv", [1, D], F32, kind="ExternalInput")
    bv_in = nc.dram_tensor("bv", [1, D], F32, kind="ExternalInput")
    out_t = nc.dram_tensor("out", [BCORE, SEQ, D], F32, kind="ExternalOutput")
    if dbg:
        dbg_y = nc.dram_tensor("dbg_y", [P, D], F32, kind="ExternalOutput")
        dbg_rows = nc.dram_tensor("dbg_rows", [P, 64], F32, kind="ExternalOutput")
        dbg_st = nc.dram_tensor("dbg_st", [P, TT], F32, kind="ExternalOutput")
        dbg_xb = nc.dram_tensor("dbg_xb", [P, TT], F32, kind="ExternalOutput")
        dbg_r2 = nc.dram_tensor("dbg_r2", [P, 64], F32, kind="ExternalOutput")
        dbg_r3 = nc.dram_tensor("dbg_r3", [P, 64], F32, kind="ExternalOutput")
        dbg_cv = nc.dram_tensor("dbg_cv", [P, 8], F32, kind="ExternalOutput")
        dbg_xnbf = nc.dram_tensor("dbg_xnbf", [P, TT], F32, kind="ExternalOutput")
        dbg_hb = nc.dram_tensor("dbg_hb", [P, TT], F32, kind="ExternalOutput")
        dbg_g3d = nc.dram_tensor("dbg_g3d", [P, TT], F32, kind="ExternalOutput")
        dbg_ffns = nc.dram_tensor("dbg_ffns", [P, TT], F32, kind="ExternalOutput")
        dbg_bufb = nc.dram_tensor("dbg_bufb", [P, D], F32, kind="ExternalOutput")
    scr = {
        1: nc.dram_tensor("scr1", [HM, P, 2 * DK * P], BF16),
        2: nc.dram_tensor("scr2", [HM, P, HK * P], BF16),
        3: nc.dram_tensor("scr3", [HM, P, HK * P], BF16),
        4: nc.dram_tensor("scr4", [DM, P, HK * P], BF16),
    }
    pks = {1: pk1, 2: pk2, 3: pk3, 4: pk4}

    es = ExitStack()
    sb = lambda name, shape, dt: es.enter_context(nc.sbuf_tensor(name, shape, dt))
    psb = lambda name, shape, dt: es.enter_context(nc.psum_tensor(name, shape, dt))

    hb = sb("hb", [P, HK * TT], BF16)
    g3 = sb("g3", [P, HK * TT], BF16)
    xnbf = sb("xnbf", [P, DK * TT], BF16)
    ffns = sb("ffns", [P, DM * TT], F32)
    bufb = sb("bufb", [P, NCH * D], F32)
    bufc = [sb(f"bufc{j}", [P, D], F32) for j in range(3)]
    x0st = [sb(f"x0st{j}", [P, D], F32) for j in range(4)]
    xnstg = [sb(f"xnstg{j}", [P, D], BF16) for j in range(2)]
    scratch = sb("scratch", [P, D], F32)
    t_tmp = [sb(f"ttmp{j}", [P, TT], F32) for j in range(2)]
    yh_tmp = [sb(f"yhtmp{j}", [P, TT], F32) for j in range(2)]
    u_tmp = [sb(f"utmp{j}", [P, TT], F32) for j in range(2)]
    states = [sb(f"states{j}", [P, TT], BF16) for j in range(2)]
    xbs = [sb(f"xbs{j}", [P, TT], F32) for j in range(2)]
    wsl = [sb(f"wsl{j}", [P, HK * P], BF16) for j in range(2)]
    gbc = sb("gbc", [P, D], F32)
    bbc = sb("bbc", [P, D], F32)
    dbc = sb("dbc", [P, D], F32)
    abf = sb("abf", [P, P], BF16)
    btbf = sb("btbf", [P, DK * P], BF16)
    ctbf = sb("ctbf", [P, D], BF16)
    idf = sb("idf", [P, P], F32)
    idb = sb("idb", [P, P], BF16)
    s1h = sb("s1h", [P, HM], F32)
    s2s = sb("s2s", [P, HM], F32)
    sh1s = sb("sh1s", [P, HM], F32)
    sh2s = sb("sh2s", [P, HM], F32)
    s3s = sb("s3s", [P, DM], F32)
    rows = sb("rows", [P, 64], F32)
    vrow = sb("vrow", [1, 3 * D], F32)
    cvals = sb("cvals", [P, 8], F32)
    junk = sb("junk", [P, 8], F32)
    lnp = [sb(f"lnp{j}", [P, 32], F32) for j in range(2)]

    pA = psb("pA", [P, 512], F32)
    pB = psb("pB", [P, 512], F32)
    pASM = psb("pASM", [P, 1024], F32)
    pSCAN = psb("pSCAN", [P, 512], F32)
    pCOUT = psb("pCOUT", [P, 1024], F32)
    pTR = psb("pTR", [P, 1024], BF16)

    semnames = ["sx", "sg", "wA", "wB", "wgA", "wgB", "wbA", "wbB",
                "x0a", "x0b", "x0c", "x0d", "soa", "sob", "soc",
                "pe", "act", "dve", "pl"]
    sems = {n: es.enter_context(nc.semaphore(n)) for n in semnames}
    S = Sched(semnames)

    def RA(par, idx, n=1):
        return rows[:, par * 32 + idx: par * 32 + idx + n]

    def mm(out_ap, lhsT_ap, rhs_ap, start, stop, waits=(), inc=None):
        def fn(o=out_ap, l=lhsT_ap, r=rhs_ap, st=start, sp=stop):
            return nc.tensor.matmul(o, l, r, start=st, stop=sp)
        return S.emit("tensor", fn, waits=waits, inc=inc)

    tok_scales = {}
    prev_init = None
    for key, src, dst in (("s1", s1_in, s1h), ("s2", s2_in, s2s),
                          ("sh1", sh1_in, sh1s), ("sh2", sh2_in, sh2s),
                          ("s3", s3_in, s3s)):
        prev_init = tok_scales[key] = S.emit(
            "sync", lambda s=src, d=dst: nc.sync.dma_start(out=d[:, :], in_=s[:, :]),
            waits=[dh(prev_init)], inc="sx", inc_by=16)
    tok_vrow = []
    for j, src in enumerate((dv_in, gv_in, bv_in)):
        prev_init = S.emit(
            "sync", lambda s=src, j=j: nc.sync.dma_start(
                out=vrow[0:1, j * D:(j + 1) * D], in_=s[:, :]),
            waits=[dh(prev_init)], inc="sx", inc_by=16)
        tok_vrow.append(prev_init)
    tok_abc = {}
    prev_g = None
    for key, src, dst in (("a", at_in, abf), ("b", bt_in, btbf), ("c", ct_in, ctbf)):
        prev_g = tok_abc[key] = S.emit(
            "gpsimd", lambda s=src, d=dst: nc.gpsimd.dma_start(out=d[:, :], in_=s[:, :]),
            waits=[dh(prev_g)], inc="sg", inc_by=16)
    tms = S.emit("gpsimd", lambda: nc.gpsimd.memset(idf[:, :], 0.0), inc="pl")
    tok_idf = S.emit("gpsimd", lambda: nc.gpsimd.affine_select(
        out=idf[:, :], in_=idf[:, :], compare_op=ALU.not_equal, fill=1.0,
        base=0, pattern=[[-1, P]], channel_multiplier=1),
        waits=[dh(tms)], inc="pl")
    tms = S.emit("gpsimd", lambda: nc.gpsimd.memset(idb[:, :], 0.0), inc="pl")
    tok_idb = S.emit("gpsimd", lambda: nc.gpsimd.affine_select(
        out=idb[:, :], in_=idb[:, :], compare_op=ALU.not_equal, fill=1.0,
        base=0, pattern=[[-1, P]], channel_multiplier=1),
        waits=[dh(tms)], inc="pl")
    tok_bc = []
    t = prev_g
    for j, dst in enumerate((dbc, gbc, bbc)):
        t = S.emit("gpsimd",
                   lambda d=dst, j=j: nc.gpsimd.dma_start(
                       out=d[0:1, :], in_=vrow[0:1, j * D:(j + 1) * D]),
                   waits=[dh(tok_vrow[j]), dh(t)], inc="sg", inc_by=16)
        k = 1
        while k < P:
            g2 = min(k, P - k)
            t = S.emit("gpsimd",
                       lambda d=dst, k=k, g2=g2: nc.gpsimd.dma_start(
                           out=d[k:k + g2, :], in_=d[0:g2, :]),
                       waits=[dh(t)], inc="sg", inc_by=16)
            k += g2
        tok_bc.append(t)
    for j, v in enumerate((0.5, 1.0 / D, -1.0, EPS, 1.0, 0.0)):
        S.emit("vector", lambda j=j, v=v: nc.vector.memset(cvals[:, j:j + 1], v),
               inc="dve")

    def dspace(waits=()):
        return S.emit("vector", lambda: nc.vector.memset(junk[:, 0:1], 0.0),
                      waits=waits, inc="dve")

    def aspace(waits=()):
        return S.emit("scalar", lambda: nc.scalar.activation(
            junk[:, 4:5], cvals[:, 5:6], AF.Copy), waits=waits, inc="act")

    S.emit("vector", lambda: nc.vector.memset(rows[:, :], 0.0), inc="dve")
    S.emit("vector", lambda: nc.vector.memset(cvals[:, 6:8], 0.0), inc="dve")
    S.emit("vector", lambda: nc.vector.memset(junk[:, :], 0.0), inc="dve")
    tok_cv = dspace()
    tok_cv = dspace()
    C_HALF, C_INVD, C_NEG1, C_EPS, C_ONE, C_ZERO = (
        cvals[:, 0:1], cvals[:, 1:2], cvals[:, 2:3], cvals[:, 3:4],
        cvals[:, 4:5], cvals[:, 5:6])
    tok_dp1 = S.emit("vector", lambda: nc.vector.tensor_scalar(
        dbc[:, :], dbc[:, :], C_ONE, None, ALU.add),
        waits=[tok_bc[0], sc(tok_cv)], inc="dve")
    tok_s1h = S.emit("vector", lambda: nc.vector.tensor_scalar(
        s1h[:, :], s1h[:, :], C_HALF, None, ALU.mult),
        waits=[tok_scales["s1"], sc(tok_cv)], inc="dve")
    tok_tbl = S.emit("scalar", lambda: nc.scalar.activation(
        junk[:, 4:5], cvals[:, 5:6], AF.Copy), waits=[sc(tok_cv)], inc="act")

    slab_ready = {}
    slab_release = {}
    last_wb = {"wbA": None, "wbB": None}
    u_ctr = [0]
    preloaded = {}

    def slab_load(tile, stage, m):
        key = (tile, stage, m)
        if key in preloaded:
            return preloaded[key]
        u = u_ctr[0]
        u_ctr[0] += 1
        buf = wsl[u % 2]
        wsem = ("wgA" if u % 2 == 0 else "wgB") if tile == 0 else \
               ("wA" if u % 2 == 0 else "wB")
        wbsem = "wbA" if u % 2 == 0 else "wbB"
        cols = 2 * DK * P if stage == 1 else HK * P
        war = slab_release.get(u - 2)
        if tile == 0:
            t = S.emit("gpsimd",
                       lambda b=buf, s=pks[stage], m=m, cols=cols:
                       nc.gpsimd.dma_start(out=b[:, 0:cols], in_=s[m]),
                       waits=[war, dh(last_wb[wbsem])], inc=wsem, inc_by=16)
            last_wb[wbsem] = S.emit(
                "gpsimd",
                lambda b=buf, s=scr[stage], m=m, cols=cols:
                nc.gpsimd.dma_start(out=s[m], in_=b[:, 0:cols]),
                waits=[dh(t)], inc=wbsem, inc_by=16)
        else:
            t = S.emit("sync",
                       lambda b=buf, s=scr[stage], m=m, cols=cols:
                       nc.sync.dma_start(out=b[:, 0:cols], in_=s[m]),
                       waits=[war, dh(last_wb["wbA"]), dh(last_wb["wbB"])],
                       inc=wsem, inc_by=16)
        slab_ready[u] = t
        preloaded[key] = (u, buf)
        return u, buf

    tok_tanh = {}
    tok_scan_mm = {}
    tok_xnbf_evac = {}
    tok_xb_evac = {}
    tok_x0_dma = {}
    tok_x0_rel = {}
    tok_stats_x0 = {}
    tok_rows_x0 = {}
    tok_rows_attn = {}
    tok_v4 = {}
    tok_trf = {}
    tok_h = {}
    tok_u = {}
    tok_yh = {}
    tok_g3 = {}
    tok_g4 = {}
    tok_f4 = {}
    tok_s2mm = {}
    tok_s3mm = {}
    tok_s4mm = {}
    tok_s1w2 = {}
    tok_cout1 = {}
    tok_cout2 = {}
    tok_ys = {}
    tok_ysq = {}
    tok_y2 = {}
    tok_dma_out = {}
    tok_xnstg = {}
    tok_xntr = {}
    tok_sq_last = {}
    tok_lnp_x0 = {}
    tok_lnp_attn = {}

    X0SEMS = ("x0a", "x0b", "x0c", "x0d")

    def x0_dma(tile):
        for c in range(NCH):
            b, t0 = c // 2, tile * STEPS + (c % 2) * P
            tok_x0_dma[(tile, c)] = S.emit(
                "sync",
                lambda c=c, b=b, t0=t0: nc.sync.dma_start(
                    out=x0st[c][:, :], in_=x_in[b, t0:t0 + P, :]),
                waits=[tok_x0_rel.get((tile - 1, c))], inc=X0SEMS[c], inc_by=16)

    def phase_a1(tile, war_toks=()):
        par = tile % 2
        toks = []
        tsq_last = None
        t2 = None
        for c in range(NCH):
            tsq = S.emit("scalar", lambda c=c: nc.scalar.activation(
                bufc[0][:, :], x0st[c][:, :], AF.Square),
                waits=[tok_x0_dma[(tile, c)], t2] + list(war_toks), inc="act")
            tsq_last = tsq
            t1 = S.emit("vector", lambda c=c, par=par: nc.vector.reduce_sum(
                RA(par, 16 + c), x0st[c][:, :], axis=mybir.AxisListType.X),
                waits=[tok_x0_dma[(tile, c)]], inc="dve")
            t2 = S.emit("vector", lambda c=c, par=par: nc.vector.reduce_sum(
                RA(par, 20 + c), bufc[0][:, :], axis=mybir.AxisListType.X),
                waits=[tsq], inc="dve")
            toks += [t1, t2]
        toks.append(dspace())
        tok_stats_x0[tile] = toks
        tok_sq_last[tile] = tsq_last

    def rows_ln(par, sum_i, sumsq_i, var_i, mu_i, deps):
        dspace(list(deps) + [sc(tok_cv)])
        S.emit("vector", lambda par=par, a=sum_i, b=mu_i: nc.vector.tensor_scalar(
            RA(par, b, 4), RA(par, a, 4), C_INVD, None, ALU.mult), inc="dve")
        tmu = dspace()
        S.emit("vector", lambda par=par, a=sum_i, b=mu_i: nc.vector.tensor_tensor(
            RA(par, a, 4), RA(par, b, 4), RA(par, b, 4), ALU.mult), inc="dve")
        dspace()
        S.emit("vector", lambda par=par, a=sum_i, b=sumsq_i, v=var_i:
               nc.vector.scalar_tensor_tensor(
                   RA(par, v, 4), RA(par, b, 4), C_INVD, RA(par, a, 4),
                   ALU.mult, ALU.subtract), inc="dve")
        tvar = dspace()
        tvar = dspace()
        return tmu, tvar

    def rows_sqrt(par, var_i, tvar):
        S.emit("scalar", lambda par=par, v=var_i: nc.scalar.activation(
            RA(par, v, 4), RA(par, v, 4), AF.Sqrt, bias=C_EPS),
            waits=[tvar, tok_cv], inc="act")
        t = aspace()
        return aspace()

    def rows_recip(par, var_i, r_i, negmr_i, mu_i, tsig, tmu):
        S.emit("vector", lambda par=par, v=var_i, r=r_i: nc.vector.reciprocal(
            RA(par, r, 4), RA(par, v, 4)), waits=[tsig], inc="dve")
        tr = dspace()
        S.emit("vector", lambda par=par, r=r_i, n=negmr_i, m=mu_i:
               nc.vector.scalar_tensor_tensor(
                   RA(par, n, 4), RA(par, m, 4), C_NEG1, RA(par, r, 4),
                   ALU.mult, ALU.mult), waits=[tmu, sc(tok_cv)], inc="dve")
        tn = dspace()
        tn = dspace()
        return tr, tn

    def ln_finalize_x0(tile):
        par = tile % 2
        S.emit("vector", lambda par=par: nc.vector.tensor_copy(
            lnp[par][:, 16:32], rows[:, par * 32 + 16: par * 32 + 32]),
            waits=[tok_rows_x0[tile][0], tok_rows_x0[tile][1],
                   tok_rows_x0[tile][2], tok_rows_x0[tile][3]], inc="dve")
        dspace()
        return dspace()

    def ln_finalize_attn(tile, tr2, tn2):
        par = tile % 2
        S.emit("vector", lambda par=par: nc.vector.tensor_copy(
            lnp[par][:, 0:8], rows[:, par * 32: par * 32 + 8]),
            waits=[tr2, tn2], inc="dve")
        dspace()
        return dspace()

    def ln_applies(tile):
        par = tile % 2
        tfin = tok_lnp_x0[tile]
        evac_last = None
        for c in range(NCH):
            S.emit("vector", lambda c=c, par=par: nc.vector.tensor_scalar(
                bufb[:, c * D:(c + 1) * D], x0st[c][:, :],
                lnp[par][:, 16 + c:17 + c], lnp[par][:, 20 + c:21 + c],
                ALU.mult, ALU.add),
                waits=[sc(tfin)], inc="dve")
            war_stg = (tok_xntr.get((tile, c - 2)) if c >= 2
                       else tok_xntr.get((tile - 1, c + 2)))
            ta2 = S.emit("vector", lambda c=c, par=par: nc.vector.tensor_scalar(
                xnstg[c % 2][:, :], x0st[c][:, :],
                lnp[par][:, 16 + c:17 + c], lnp[par][:, 20 + c:21 + c],
                ALU.mult, ALU.add),
                waits=[war_stg], inc="dve")
            tok_x0_rel[(tile, c)] = ta2
            tok_xnstg[(tile, c)] = ta2
            ttr = None
            for f in range(DK):
                w = []
                if f == 0:
                    war_ptr = evac_last if c > 0 else tok_xnbf_evac.get(tile - 1)
                    w = [tok_xnstg[(tile, c)], tok_idb, war_ptr]
                def fnt(c=c, f=f): return nc.tensor.transpose(
                    pTR[:, f * P:(f + 1) * P],
                    xnstg[c % 2][:, f * P:(f + 1) * P], idb[:, :])
                ttr = S.emit("tensor", fnt, waits=w,
                             inc="pe" if f == DK - 1 else None)
            tok_xntr[(tile, c)] = ttr
            evac_last = S.emit(
                "vector",
                lambda c=c: nc.vector.tensor_copy(
                    xnbf[:, :].rearrange("p (f t) -> p f t", f=DK)[:, :, c * P:(c + 1) * P],
                    pTR[:, :].rearrange("p (f t) -> p f t", f=DK)),
                waits=[ttr,
                       tok_s1w2.get(tile - 1) if c == 0 else None],
                inc="dve")
        tok_xnbf_evac[tile] = evac_last
        txb = None
        for k in range(DK):
            w = []
            if k == 0:
                w = [evac_last, tok_abc["b"], tok_y2.get((tile - 1, NCH - 1))]
            txb = mm(pCOUT[:, 0:512], btbf[:, k * P:(k + 1) * P],
                     xnbf[:, k * TT:(k + 1) * TT], k == 0, k == DK - 1,
                     waits=w, inc="pe" if k == DK - 1 else None)
        tok_xb_evac[tile] = S.emit(
            "scalar", lambda par=par: nc.scalar.activation(
                xbs[par][:, :], pCOUT[:, 0:512], AF.Copy),
            waits=[txb], inc="act")

    def emit_scan_step(tile, s):
        par = tile % 2
        if tile == 0 and s == 0:
            tk = None
            for b in (0, 1):
                tk = S.emit("scalar", lambda b=b: nc.scalar.activation(
                    states[0][:, b * STEPS: b * STEPS + 1],
                    xbs[0][:, b * STEPS: b * STEPS + 1], AF.Tanh),
                    waits=[sc(tok_xb_evac[0]), tok_tbl], inc="act")
            tok_tanh[(0, 0)] = tk
            return
        if s == 0:
            prev, col0, wt = states[1 - par], STEPS - 1, tok_tanh[(tile - 1, STEPS - 1)]
        else:
            prev, col0, wt = states[par], s - 1, tok_tanh[(tile, s - 1)]
        w = [wt]
        if s == 0:
            w += [tok_abc["a"],
                  tok_cout2.get((tile - 2, NCH - 1)),
                  tok_scan_mm.get((tile - 1, 0))]
        tmm = mm(pSCAN[:, 0:2], abf[:, :], prev[:, col0::STEPS], True, True,
                 waits=w, inc="pe")
        tok_scan_mm[(tile, s)] = tmm
        tk = None
        for b in (0, 1):
            tk = S.emit("scalar", lambda par=par, s=s, b=b: nc.scalar.activation(
                states[par][:, b * STEPS + s: b * STEPS + s + 1],
                pSCAN[:, b:b + 1], AF.Tanh,
                bias=xbs[par][:, b * STEPS + s: b * STEPS + s + 1]),
                waits=[tmm, sc(tok_xb_evac[tile])], inc="act")
        tok_tanh[(tile, s)] = tk

    TOTAL_MM = HM * 2 * DK + HM * HK + HM * HK + DM * HK

    def tile_body(i):
        par = i % 2
        mm_done = [0]
        scan_next = [0]

        def pump_scan(force=False):
            due = STEPS if force else (mm_done[0] * STEPS) // TOTAL_MM
            while scan_next[0] < min(due, STEPS):
                emit_scan_step(i, scan_next[0])
                scan_next[0] += 1

        for m in range(HM):
            u, buf = slab_load(i, 1, m)
            tok_p1 = None
            for k in range(DK):
                w = []
                if k == 0:
                    w = [dh(slab_ready[u]), tok_yh.get((i, m - 1)), tok_xnbf_evac[i]]
                t = mm(pA[:, :], buf[:, k * P:(k + 1) * P],
                       xnbf[:, k * TT:(k + 1) * TT], k == 0, k == DK - 1,
                       waits=w, inc="pe" if k == DK - 1 else None)
                if t:
                    tok_p1 = t
            tok_p2 = None
            for k in range(DK):
                w = []
                if k == 0:
                    w = [tok_h.get((i, m - 1))]
                t = mm(pB[:, :], buf[:, (DK + k) * P:(DK + k + 1) * P],
                       xnbf[:, k * TT:(k + 1) * TT], k == 0, k == DK - 1,
                       waits=w, inc="pe" if k == DK - 1 else None)
                if t:
                    tok_p2 = t
            slab_release[u] = tok_p2
            if m == HM - 1:
                tok_s1w2[i] = tok_p2
            mp = m % 2
            tt, yy, uu = t_tmp[mp], yh_tmp[mp], u_tmp[mp]
            S.emit("scalar", lambda m=m, tt=tt: nc.scalar.activation(
                tt[:, :], pA[:, :], AF.Tanh, scale=s1h[:, m:m + 1]),
                waits=[tok_p1, tok_u.get((i, m - 2)), sc(tok_s1h)], inc="act")
            tok_yh[(i, m)] = S.emit("scalar", lambda m=m, yy=yy: nc.scalar.activation(
                yy[:, :], pA[:, :], AF.Copy, scale=s1h[:, m:m + 1]), inc="act")
            tok_u[(i, m)] = S.emit(
                "vector", lambda tt=tt, yy=yy, uu=uu: nc.vector.scalar_tensor_tensor(
                    uu[:, :], tt[:, :], 1.0, yy[:, :], ALU.add, ALU.mult),
                waits=[tok_yh[(i, m)]], inc="dve")
            tok_h[(i, m)] = S.emit(
                "vector", lambda m=m, uu=uu: nc.vector.scalar_tensor_tensor(
                    hb[:, m * TT:(m + 1) * TT], pB[:, :], s2s[:, m:m + 1],
                    uu[:, :], ALU.mult, ALU.mult),
                waits=[tok_p2, sc(tok_scales["s2"]),
                       tok_s4mm.get((i - 1, DM - 1))], inc="dve")
            mm_done[0] += 2 * DK
            pump_scan()
            if m == 1 and i + 1 < nt:
                x0_dma(i + 1)
                phase_a1(i + 1, war_toks=[
                    tok_v4.get((i - 1, NCH - 1)),
                    dh(tok_dma_out.get((i - 1, 1)))])

        for m in range(HM):
            u, buf = slab_load(i, 2, m)
            dst = pA if m % 2 == 0 else pB
            if m == 0:
                tok_free = tok_yh[(i, HM - 1)]
            elif m == 1:
                tok_free = tok_h[(i, HM - 1)]
            else:
                tok_free = tok_g3[(i, m - 2)]
            tokmm = None
            for k in range(HK):
                w = []
                if k == 0:
                    w = [dh(slab_ready[u]), tok_free, tok_h[(i, HK - 1)]]
                t = mm(dst[:, :], buf[:, k * P:(k + 1) * P],
                       hb[:, k * TT:(k + 1) * TT], k == 0, k == HK - 1,
                       waits=w, inc="pe" if k == HK - 1 else None)
                if t:
                    tokmm = t
            slab_release[u] = tokmm
            tok_s2mm[(i, m)] = tokmm
            tok_g3[(i, m)] = S.emit(
                "scalar", lambda m=m, dst=dst: nc.scalar.activation(
                    g3[:, m * TT:(m + 1) * TT], dst[:, :], AF.Gelu,
                    scale=sh1s[:, m:m + 1]),
                waits=[tokmm, sc(tok_scales["sh1"]),
                       tok_s3mm.get((i - 1, HM - 1))], inc="act")
            mm_done[0] += HK
            pump_scan()

        for m in range(HM):
            u, buf = slab_load(i, 3, m)
            dst = pA if m % 2 == 0 else pB
            if m == 0:
                tok_free = tok_g3[(i, HM - 2)]
            elif m == 1:
                tok_free = tok_g3[(i, HM - 1)]
            else:
                tok_free = tok_g4[(i, m - 2)]
            tokmm = None
            for k in range(HK):
                w = []
                if k == 0:
                    w = [dh(slab_ready[u]), tok_free, tok_g3[(i, HM - 1)]]
                t = mm(dst[:, :], buf[:, k * P:(k + 1) * P],
                       g3[:, k * TT:(k + 1) * TT], k == 0, k == HK - 1,
                       waits=w, inc="pe" if k == HK - 1 else None)
                if t:
                    tokmm = t
            slab_release[u] = tokmm
            tok_s3mm[(i, m)] = tokmm
            tok_g4[(i, m)] = S.emit(
                "scalar", lambda m=m, dst=dst: nc.scalar.activation(
                    hb[:, m * TT:(m + 1) * TT], dst[:, :], AF.Gelu,
                    scale=sh2s[:, m:m + 1]),
                waits=[tokmm, sc(tok_scales["sh2"])], inc="act")
            mm_done[0] += HK
            pump_scan()

        for m in range(DM):
            u, buf = slab_load(i, 4, m)
            dst = pA if m % 2 == 0 else pB
            if m == 0:
                tok_free = tok_g4[(i, HM - 2)]
            elif m == 1:
                tok_free = tok_g4[(i, HM - 1)]
            else:
                tok_free = tok_f4[(i, m - 2)]
            tokmm = None
            for k in range(HK):
                w = []
                if k == 0:
                    w = [dh(slab_ready[u]), tok_free, tok_g4[(i, HM - 1)]]
                t = mm(dst[:, :], buf[:, k * P:(k + 1) * P],
                       hb[:, k * TT:(k + 1) * TT], k == 0, k == HK - 1,
                       waits=w, inc="pe" if k == HK - 1 else None)
                if t:
                    tokmm = t
            slab_release[u] = tokmm
            tok_s4mm[(i, m)] = tokmm
            tok_f4[(i, m)] = S.emit(
                "scalar", lambda m=m, dst=dst: nc.scalar.activation(
                    ffns[:, m * TT:(m + 1) * TT], dst[:, :], AF.Copy,
                    scale=s3s[:, m:m + 1]),
                waits=[tokmm, sc(tok_scales["s3"]),
                       tok_trf.get((i - 1, NCH - 1))], inc="act")
            mm_done[0] += HK
            pump_scan()
        pump_scan(force=True)

        for c in range(NCH):
            wt_tanh = tok_tanh[(i, P - 1 if c % 2 == 0 else STEPS - 1)]
            tcm = None
            for half in range(2):
                w = []
                if half == 0:
                    w = [wt_tanh, tok_abc["c"],
                         tok_ys.get((i, c - 1)), tok_xb_evac[i]]
                tcm = mm(pCOUT[:, half * 512:(half + 1) * 512],
                         states[par][:, c * P:(c + 1) * P],
                         ctbf[:, half * 512:(half + 1) * 512], True, True,
                         waits=w, inc="pe" if half == 1 else None)
            tok_cout1[(i, c)] = tcm
            S.emit("vector", lambda c=c: nc.vector.tensor_tensor(
                bufc[0][:, :], bufb[:, c * D:(c + 1) * D], dbc[:, :], ALU.mult),
                waits=[tok_dp1, tok_dma_out.get((i - 1, NCH - 1)),
                       tok_sq_last.get(i + 1)], inc="dve")
            tok_ys[(i, c)] = S.emit(
                "vector", lambda c=c: nc.vector.scalar_tensor_tensor(
                    scratch[:, :], pCOUT[:, :], 0.0, bufc[0][:, :],
                    ALU.add, ALU.add),
                waits=[tcm], inc="dve")
            S.emit("vector", lambda c=c, par=par: nc.vector.reduce_sum(
                RA(par, 0 + c), scratch[:, :], axis=mybir.AxisListType.X),
                inc="dve")
            tsq = S.emit("scalar", lambda: nc.scalar.activation(
                bufc[0][:, :], scratch[:, :], AF.Square),
                waits=[tok_ys[(i, c)]], inc="act")
            tok_ysq[(i, c)] = S.emit(
                "vector", lambda c=c, par=par: nc.vector.reduce_sum(
                    RA(par, 4 + c), bufc[0][:, :], axis=mybir.AxisListType.X),
                waits=[tsq], inc="dve")
            if c == NCH - 1:
                tok_ysq[(i, c)] = dspace()
            if dbg and i == 0 and c == 3:
                S.emit("sync", lambda: nc.sync.dma_start(
                    out=dbg_y[:, :], in_=scratch[:, :]),
                    waits=[tok_ysq[(i, c)]], inc="sx", inc_by=16)

        tmu2, tvar2 = rows_ln(par, 0, 4, 12, 8,
                              [sc(tok_ysq[(i, c)]) for c in range(NCH)])
        tsig2 = rows_sqrt(par, 12, tvar2)
        if i + 1 < nt:
            q = (i + 1) % 2
            tmu0, tvar0 = rows_ln(q, 16, 20, 24, 28,
                                  [sc(t) for t in tok_stats_x0[i + 1]])
            tsig0 = rows_sqrt(q, 24, tvar0)
            tr0, tn0 = rows_recip(q, 24, 16, 20, 28, tsig0, tmu0)
            tok_rows_x0[i + 1] = (tr0, tn0, tsig0, tmu0)
            tok_lnp_x0[i + 1] = ln_finalize_x0(i + 1)
        tr2, tn2 = rows_recip(par, 12, 0, 4, 8, tsig2, tmu2)
        tok_rows_attn[i] = (tr2, tn2)
        tok_lnp_attn[i] = ln_finalize_attn(i, tr2, tn2)
        if dbg and i == 0:
            S.emit("sync", lambda: nc.sync.dma_start(
                out=dbg_rows[:, :], in_=rows[:, :]), waits=[tn2],
                inc="sx", inc_by=16)
            S.emit("gpsimd", lambda: nc.gpsimd.dma_start(
                out=dbg_xnbf[:, :], in_=xnbf[:, 0:TT]),
                waits=[tok_xnbf_evac[0]], inc="sg", inc_by=16)
            S.emit("gpsimd", lambda: nc.gpsimd.dma_start(
                out=dbg_hb[:, :], in_=hb[:, 0:TT]),
                waits=[tok_h[(0, 0)]], inc="sg", inc_by=16)
            S.emit("gpsimd", lambda: nc.gpsimd.dma_start(
                out=dbg_g3d[:, :], in_=g3[:, 0:TT]),
                waits=[tok_g3[(0, 0)]], inc="sg", inc_by=16)
            S.emit("sync", lambda: nc.sync.dma_start(
                out=dbg_ffns[:, :], in_=ffns[:, 0:TT]),
                waits=[tok_f4[(0, 0)]], inc="sx", inc_by=16)
            S.emit("sync", lambda: nc.sync.dma_start(
                out=dbg_bufb[:, :], in_=bufb[:, 0:D]),
                waits=[tok_xnbf_evac[0]], inc="sx", inc_by=16)
            S.emit("gpsimd", lambda: nc.gpsimd.dma_start(
                out=dbg_st[:, :], in_=states[0][:, :]),
                waits=[tok_tanh[(0, STEPS - 1)]], inc="sg", inc_by=16)
            S.emit("sync", lambda: nc.sync.dma_start(
                out=dbg_xb[:, :], in_=xbs[0][:, :]),
                waits=[tok_xb_evac[0]], inc="sx", inc_by=16)

        sig0tok = mu0tok = tok_lnp_x0[i]
        tfin2 = tok_lnp_attn[i]
        for c in range(NCH):
            ttr = None
            for f in range(DM):
                w = []
                if f == 0:
                    w = [tok_f4[(i, DM - 1)], tok_idf,
                         tok_v4.get((i, c - 1)) if c > 0 else
                         tok_v4.get((i - 1, NCH - 1))]
                def fnt(c=c, f=f): return nc.tensor.transpose(
                    pASM[:, f * P:(f + 1) * P],
                    ffns[:, f * TT + c * P: f * TT + c * P + P], idf[:, :])
                ttr = S.emit("tensor", fnt, waits=w,
                             inc="pe" if f == DM - 1 else None)
            tok_trf[(i, c)] = ttr
            tcm = None
            for half in range(2):
                w = []
                if half == 0:
                    w = [tok_ys[(i, NCH - 1)] if c == 0 else tok_y2.get((i, c - 1))]
                tcm = mm(pCOUT[:, half * 512:(half + 1) * 512],
                         states[par][:, c * P:(c + 1) * P],
                         ctbf[:, half * 512:(half + 1) * 512], True, True,
                         waits=w, inc="pe" if half == 1 else None)
            tok_cout2[(i, c)] = tcm
            a, b_, f_ = bufc[c % 3], bufc[(c + 1) % 3], bufc[(c + 2) % 3]
            war_a = (tok_dma_out.get((i, c - 2)) if c >= 2
                     else tok_dma_out.get((i - 1, NCH - 1)))
            war_b = (tok_dma_out.get((i, c - 1)) if c >= 1
                     else tok_dma_out.get((i - 1, NCH - 1)))
            war_f = (tok_dma_out.get((i, c - 3)) if c >= 3
                     else tok_dma_out.get((i - 1, NCH - 1)))
            S.emit("vector", lambda c=c, a=a: nc.vector.tensor_tensor(
                a[:, :], bufb[:, c * D:(c + 1) * D], dbc[:, :], ALU.mult),
                waits=[war_a], inc="dve")
            ty = S.emit("vector", lambda a=a, b_=b_: nc.vector.scalar_tensor_tensor(
                b_[:, :], pCOUT[:, :], 0.0, a[:, :], ALU.add, ALU.add),
                waits=[tcm, war_b], inc="dve")
            tok_y2[(i, c)] = ty
            S.emit("vector", lambda c=c, par=par, a=a, b_=b_: nc.vector.tensor_scalar(
                a[:, :], b_[:, :], lnp[par][:, 0 + c:1 + c], lnp[par][:, 4 + c:5 + c],
                ALU.mult, ALU.add),
                waits=[sc(tfin2)], inc="dve")
            S.emit("vector", lambda a=a, b_=b_: nc.vector.tensor_tensor(
                b_[:, :], a[:, :], gbc[:, :], ALU.mult),
                waits=[tok_bc[1]], inc="dve")
            S.emit("vector", lambda a=a, b_=b_: nc.vector.tensor_tensor(
                a[:, :], b_[:, :], bbc[:, :], ALU.add),
                waits=[tok_bc[2]], inc="dve")
            S.emit("vector", lambda c=c, par=par, a=a, b_=b_: nc.vector.scalar_tensor_tensor(
                b_[:, :], bufb[:, c * D:(c + 1) * D], lnp[par][:, 24 + c:25 + c],
                a[:, :], ALU.mult, ALU.add),
                waits=[sc(sig0tok)], inc="dve")
            tv4 = S.emit("vector", lambda c=c, par=par, b_=b_, f_=f_:
                         nc.vector.scalar_tensor_tensor(
                             f_[:, :], b_[:, :], lnp[par][:, 28 + c:29 + c], pASM[:, :],
                             ALU.add, ALU.add),
                         waits=[sc(mu0tok), ttr, war_f], inc="dve")
            tok_v4[(i, c)] = tv4
            b, t0 = c // 2, i * STEPS + (c % 2) * P
            tok_dma_out[(i, c)] = S.emit(
                "gpsimd", lambda b=b, t0=t0, f_=f_: nc.gpsimd.dma_start(
                    out=out_t[b, t0:t0 + P, :], in_=f_[:, :]),
                waits=[tv4], inc=("soa", "sob", "soc")[(c + 2) % 3], inc_by=16)

        if i + 1 < nt:
            ln_applies(i + 1)
            slab_load(i + 1, 1, 0)
            slab_load(i + 1, 1, 1)

    x0_dma(0)
    phase_a1(0)
    if dbg:
        S.emit("sync", lambda: nc.sync.dma_start(
            out=dbg_r2[:, :], in_=rows[:, :]),
            waits=[tok_stats_x0[0][-1]], inc="sx", inc_by=16)
        S.emit("sync", lambda: nc.sync.dma_start(
            out=dbg_cv[:, :], in_=cvals[:, :]),
            waits=[sc(tok_cv)], inc="sx", inc_by=16)
    tmu0, tvar0 = rows_ln(0, 16, 20, 24, 28, tok_stats_x0[0])
    if dbg:
        S.emit("sync", lambda: nc.sync.dma_start(
            out=dbg_r3[:, :], in_=rows[:, :]), waits=[tvar0], inc="sx", inc_by=16)
    tsig0 = rows_sqrt(0, 24, tvar0)
    tr0, tn0 = rows_recip(0, 24, 16, 20, 28, tsig0, tmu0)
    tok_rows_x0[0] = (tr0, tn0, tsig0, tmu0)
    tok_lnp_x0[0] = ln_finalize_x0(0)
    ln_applies(0)

    for i in range(nt):
        tile_body(i)

    S.emit("gpsimd", lambda: nc.gpsimd.nop(),
           waits=[(sn, S.cnt[sn], "none", "d")
                  for sn in ("soa", "sob", "soc") if S.cnt[sn] > 0])

    blk = es.enter_context(nc.Block())
    eng_objs = {"sync": nc.sync, "gpsimd": nc.gpsimd, "scalar": nc.scalar,
                "vector": nc.vector, "tensor": nc.tensor}

    def make_body(name):
        def body(h):
            eng = eng_objs[name]
            for (wl, fn, incname, incby) in S.ops[name]:
                for (sname, v) in wl:
                    eng.wait_ge(sems[sname], v)
                inst = fn()
                if incname is not None:
                    inst.then_inc(sems[incname], incby)
        return body

    for name in ENGINES:
        getattr(blk, name)(make_body(name))

    es.close()
    return nc


def _pack_weights(w, nch):
    o, i = w.shape
    return np.ascontiguousarray(
        w.reshape(o // P, P, i // P, P).transpose(0, 3, 2, 1).reshape(o // P, P, i))


def make_inputs(inp):
    f32c = lambda a: np.ascontiguousarray(np.asarray(a, dtype=np.float32))
    x = f32c(inp["L0_attn_x"])
    A = f32c(inp["L0_attn_A"])
    B = f32c(inp["L0_attn_B"])
    C = f32c(inp["L0_attn_C"])
    Dv = f32c(inp["L0_attn_D"]).reshape(1, D)
    gv = f32c(inp["L0_attn_gamma"]).reshape(1, D)
    bv = f32c(inp["L0_attn_beta"]).reshape(1, D)
    w1p = _pack_weights(f32c(inp["L0_ffn_w1_w"]), DK)
    w2p = _pack_weights(f32c(inp["L0_ffn_w2_w"]), DK)
    pk1 = np.ascontiguousarray(np.concatenate([w1p, w2p], axis=2))
    pk2 = _pack_weights(f32c(inp["L0_ffn_h1_w"]), HK)
    pk3 = _pack_weights(f32c(inp["L0_ffn_h2_w"]), HK)
    pk4 = _pack_weights(f32c(inp["L0_ffn_w3_w"]), HK)
    sc_ = lambda s: np.ascontiguousarray(
        np.asarray(s, dtype=np.float32).reshape(-1, P).T)
    common = {
        "pk1": pk1, "pk2": pk2, "pk3": pk3, "pk4": pk4,
        "s1": sc_(inp["L0_ffn_w1_s"]), "s2": sc_(inp["L0_ffn_w2_s"]),
        "sh1": sc_(inp["L0_ffn_h1_s"]), "sh2": sc_(inp["L0_ffn_h2_s"]),
        "s3": sc_(inp["L0_ffn_w3_s"]),
        "at": np.ascontiguousarray(A.T),
        "bt": np.ascontiguousarray(
            B.T.reshape(DK, P, P).transpose(1, 0, 2).reshape(P, DK * P)),
        "ct": np.ascontiguousarray(C.T),
        "dv": Dv, "gv": gv, "bv": bv,
    }
    in_maps = []
    for c in range(NCORES):
        m = dict(common)
        m["x"] = np.ascontiguousarray(x[BCORE * c: BCORE * (c + 1)])
        in_maps.append(m)
    return in_maps


def kernel(**inputs):
    in_maps = make_inputs(inputs)
    nc = build_program(NT)
    res = run_bass_kernel_spmd(nc, in_maps, core_ids=list(range(NCORES)))
    return np.concatenate(
        [np.asarray(r["out"], dtype=np.float32) for r in res.results], axis=0)
